# version 02 (final)
# baseline (speedup 1.0000x reference)
import numpy as np
import concourse.bass as bass
import concourse.mybir as mybir
from concourse.bass_utils import run_bass_kernel_spmd

F32 = mybir.dt.float32
AF = mybir.ActivationFunctionType
ALU = mybir.AluOpType

PH = 3
DBG = False
D = 2048
T = 2048
NCH = 16
TT = 512
NT = T // TT
TB = 256
NTB = T // TB
NE = 8
HCH = 8
NSEQ = 8192.0
EPS_LN = 1e-5
EPS_ATT = 1e-6


class Ctr:
    def __init__(self, nc, ctx, name):
        self.h = ctx.enter_context(nc.semaphore(name))
        self.n = 0

    def mark(self, k=1):
        self.n += k
        return (self.h, self.n)


class Prog:
    def __init__(self):
        self.q = {k: [] for k in "tvagy"}

    def emit(self, key, fn):
        self.q[key].append(fn)

    def wait(self, key, evt):
        if isinstance(evt, Ctr):
            sem, v = evt.h, evt.n
        else:
            sem, v = evt
        if v > 0:
            self.emit(key, lambda e, sem=sem, v=v: e.wait_ge(sem, v))


def build_program():
    nc = bass.Bass("TRN2", target_bir_lowering=False,
                   detect_race_conditions=False,
                   dynamic_dma_scratch_size=8192)
    ctx = nc.ctx

    din = lambda n, s: nc.dram_tensor(n, s, F32, kind="ExternalInput").ap()
    x_in = din("x", [T, D])
    wq_d = din("wq", [NCH, 128, NCH * 128])
    wo_d = din("wo", [NCH, 128, NCH * 128])
    wk_d = din("wk", [NCH + 1, 4, 128, 512])
    wv_d = din("wv", [NCH + 1, 4, 128, 512])
    wg_d = din("wg", [NE, HCH, 128, NCH * 128])
    wu_d = din("wu", [NE, HCH, 128, NCH * 128])
    wd_d = din("wd", [NE, NCH, 128, HCH * 128])
    sg_d = din("sg", [NCH, 128, NCH * 128])
    su_d = din("su", [NCH, 128, NCH * 128])
    sd_d = din("sd", [NCH, 128, NCH * 128])
    gw_d = din("gw", [128, NCH, 8])
    se_d = din("se", [128, NCH])
    bqc_d = din("bqc", [128, NCH])
    boc_d = din("boc", [128, NCH])
    poff_d = din("poff", [1, 1])
    y_out = nc.dram_tensor("y", [T, D], F32, kind="ExternalOutput").ap()
    dbg_out = (nc.dram_tensor("dbg", [6, 128, 4096], F32,
                              kind="ExternalOutput").ap() if DBG else None)
    cc_in = nc.dram_tensor("cc_in", [128, NCH * 129], F32,
                           kind="Internal").ap()
    cc_out = nc.dram_tensor("cc_out", [128, NCH * 129], F32,
                            kind="Internal").ap()

    sb = lambda n, s: ctx.enter_context(nc.sbuf_tensor(n, s, F32))
    x_res = sb("x_res", [128, NCH * T])
    r_ln = sb("r_ln", [128, NCH * TT])
    r_h = sb("r_h", [128, HCH * TT + 16])
    kv = sb("kv", [128, NCH * 129])
    wb = [sb(f"wb{i}", [128, NCH * 128]) for i in range(2)]
    ones1 = sb("ones1", [1, 128])
    onescol = sb("onescol", [128, 1])
    ident = sb("ident", [128, 128])
    gwt = sb("gwt", [128, NCH * 8])
    set_ = sb("sett", [128, NCH])
    bqct = sb("bqct", [128, NCH])
    boct = sb("boct", [128, NCH])
    pofft = sb("pofft", [1, 1])
    pbias = sb("pbias", [1, 1])
    pbias_col = sb("pbias_col", [128, 1])
    dec_col = sb("dec_col", [128, NCH])
    rst_col = sb("rst_col", [128, NCH])
    nmr_col = sb("nmr_col", [128, NCH])
    sc_tok = sb("sc_tok", [128, 8])
    m8 = sb("m8", [128, 8])
    d12 = sb("d12", [128, 1])
    w1c = sb("w1c", [128, 1])
    w2c = sb("w2c", [128, 1])
    indb = sb("indb", [128, 8])
    cmbt = sb("cmbt", [128, 8])
    comb = sb("comb", [8, TT])
    row2 = sb("row2", [1, 512])
    zrow = sb("zrow", [1, 512])
    sgrep = sb("sgrep", [128, 256])
    tsc = sb("tsc", [128, 256])

    p = Prog()

    class PBank:
        def __init__(self, i):
            self.t = ctx.enter_context(nc.psum_tensor(f"pb{i}", [128, 512],
                                                      F32))
            self.prod = Ctr(nc, ctx, f"pp{i}")
            self.free_evt = None

        def begin(self):
            if self.free_evt is not None:
                p.wait("t", self.free_evt)

        def freed_by(self, evt):
            self.free_evt = evt

    B = [PBank(i) for i in range(8)]

    c_ld = Ctr(nc, ctx, "c_ld")
    c_w = [Ctr(nc, ctx, f"c_w{i}") for i in range(2)]
    c_sq = Ctr(nc, ctx, "c_sq")
    c_row = Ctr(nc, ctx, "c_row")
    c_ln = Ctr(nc, ctx, "c_ln")
    c_hs = Ctr(nc, ctx, "c_hs")
    c_vg = Ctr(nc, ctx, "c_vg")
    c_ag = Ctr(nc, ctx, "c_ag")
    c_gg = Ctr(nc, ctx, "c_gg")
    c_tg = Ctr(nc, ctx, "c_tg")
    c_cc = Ctr(nc, ctx, "c_cc")
    c_cd = Ctr(nc, ctx, "c_cd")
    c_mmu = Ctr(nc, ctx, "c_mmu")
    c_st = Ctr(nc, ctx, "c_st")

    def vself(emit_fn):
        vm = c_vg.mark(1)
        p.emit("v", lambda e, sem=vm[0]: emit_fn(e).then_inc(sem, 1))
        p.wait("v", c_vg)
        return vm

    wfree = [None, None]
    wnext = [0]

    def wload(src_ap, cols):
        i = wnext[0]
        wnext[0] ^= 1
        if wfree[i] is not None:
            p.wait("y", wfree[i])
        ld = c_w[i].mark(16)
        p.emit("y", lambda e, s=src_ap, b=wb[i], c=cols, sem=ld[0], v=16:
               e.dma_start(out=b[:, 0:c], in_=s).then_inc(sem, v))
        return i, ld

    lm = c_ld.mark(16)
    p.emit("g", lambda e, sem=lm[0]: e.dma_start(
        out=gwt.ap().rearrange("p (c k) -> p c k", c=NCH),
        in_=gw_d[:, :, :]).then_inc(sem, 16))
    for s, d in ((se_d, set_), (bqc_d, bqct), (boc_d, boct),
                 (poff_d, pofft)):
        lm = c_ld.mark(16)
        p.emit("g", lambda e, s=s, d=d, sem=lm[0]: e.dma_start(
            out=d[:, :], in_=s[:, :]).then_inc(sem, 16))

    p.emit("v", lambda e: e.memset(ones1[:], 1.0))
    p.emit("v", lambda e: e.memset(zrow[:], 0.0))
    vm0 = c_vg.mark(1)
    p.emit("v", lambda e, sem=vm0[0]: e.memset(
        onescol[:], 1.0).then_inc(sem, 1))
    gm0 = c_gg.mark(1)
    p.emit("g", lambda e, sem=gm0[0]: e.iota(
        ident[:], pattern=[[1, 128]], base=0, channel_multiplier=-1,
        allow_small_or_imprecise_dtypes=True).then_inc(sem, 1))
    p.wait("v", c_gg)
    idm = c_vg.mark(1)
    p.emit("v", lambda e, sem=idm[0]: e.tensor_scalar(
        ident[:], ident[:], 0.0, None, op0=ALU.is_equal).then_inc(sem, 1))
    ident_ready = idm
    gm = c_gg.mark(1)
    p.emit("g", lambda e, sem=gm[0]: e.iota(
        dec_col[:], pattern=[[128, NCH]], base=0, channel_multiplier=1,
        allow_small_or_imprecise_dtypes=True).then_inc(sem, 1))
    p.wait("a", c_ld)
    p.wait("a", c_gg)
    am = c_ag.mark(1)
    p.emit("a", lambda e, sem=am[0], v=am[1]: e.mul(
        pbias[:], pofft[:], -1.0 / NSEQ).then_inc(sem, 1))
    p.wait("a", c_ag)
    b7s = B[7]
    b7s.begin()
    p.wait("t", c_ag)
    pm7s = b7s.prod.mark(1)
    p.emit("t", lambda e, sem=pm7s[0]: e.matmul(
        b7s.t[:, 0:1], ones1[:], pbias[:, :], start=True,
        stop=True).then_inc(sem, 1))
    p.wait("v", b7s.prod)
    pcm = c_vg.mark(1)
    p.emit("v", lambda e, sem=pcm[0]: e.tensor_copy(
        pbias_col[:, :], b7s.t[:, 0:1]).then_inc(sem, 1))
    b7s.freed_by(pcm)
    p.wait("a", c_vg)
    am2 = c_ag.mark(1)
    p.emit("a", lambda e, sem=am2[0]: e.activation(
        dec_col[:], dec_col[:], AF.Exp, bias=pbias_col[:, :],
        scale=-1.0 / NSEQ).then_inc(sem, 1))
    decay_ready = am2

    for tb in range(NCH):
        ldm = c_ld.mark(16)
        if tb > 0:
            p.wait("g", (c_tg.h, c_tg.n))
        p.emit("g", lambda e, tb=tb, sem=ldm[0], v=ldm[1]: e.dma_start(
            out=r_h[:, 0:D], in_=x_in[tb * 128:(tb + 1) * 128, :])
            .then_inc(sem, 16))
        p.wait("t", (c_ld.h, c_ld.n))
        p.wait("t", ident_ready)
        for mc in range(NCH):
            bk = B[mc % 2]
            bk.begin()
            pm = bk.prod.mark(1)
            p.emit("t", lambda e, mc=mc, sem=pm[0], bk=bk: e.transpose(
                bk.t[:, 0:128], r_h[:, mc * 128:(mc + 1) * 128],
                ident[:]).then_inc(sem, 1))
            p.wait("v", bk.prod)
            vm = c_vg.mark(1)
            p.emit("v", lambda e, bk=bk, mc=mc, tb=tb, sem=vm[0]:
                   e.tensor_copy(
                       x_res[:, mc * T + tb * 128: mc * T + (tb + 1) * 128],
                       bk.t[:, 0:128]).then_inc(sem, 1))
            bk.freed_by(vm)
        p.wait("t", c_vg)
        tm = c_tg.mark(1)
        p.emit("t", lambda e, sem=tm[0]: e.nop().then_inc(sem, 1))
    x_ready = c_vg.mark(0)
    x_ready = (c_vg.h, c_vg.n)

    def xr(mc, t0, tw):
        return x_res[:, mc * T + t0: mc * T + t0 + tw]

    last_xw = [x_ready]

    def ln_stats_and_ln(t0, tw, rowbase, ereaders=None):
        rt_, c0 = rowbase
        if ereaders is not None:
            p.wait("a", ereaders)
        mu = rt_[0:1, c0:c0 + tw]
        var = rt_[0:1, c0 + tw:c0 + 2 * tw]
        rst = rt_[0:1, c0 + 2 * tw:c0 + 3 * tw]
        nmr = rt_[0:1, c0 + 3 * tw:c0 + 4 * tw]
        p.wait("a", last_xw[0])
        for mc in range(NCH):
            sq = c_sq.mark(1) if mc == NCH - 1 else None
            if sq is None:
                p.emit("a", lambda e, mc=mc, tw=tw, t0=t0: e.activation(
                    r_ln[:, mc * tw:(mc + 1) * tw], xr(mc, t0, tw),
                    AF.Square))
            else:
                p.emit("a", lambda e, mc=mc, tw=tw, t0=t0, sem=sq[0]:
                       e.activation(
                           r_ln[:, mc * tw:(mc + 1) * tw], xr(mc, t0, tw),
                           AF.Square).then_inc(sem, 1))
        b0, b1 = B[0], B[1]
        b0.begin()
        b1.begin()
        p.wait("t", last_xw[0])
        p.wait("t", c_sq)
        for mc in range(NCH):
            last = mc == NCH - 1
            if not last:
                p.emit("t", lambda e, mc=mc, tw=tw, t0=t0: e.matmul(
                    b0.t[0:1, 0:tw], onescol[:], xr(mc, t0, tw),
                    start=(mc == 0), stop=False))
                p.emit("t", lambda e, mc=mc, tw=tw: e.matmul(
                    b1.t[0:1, 0:tw], onescol[:],
                    r_ln[:, mc * tw:(mc + 1) * tw],
                    start=(mc == 0), stop=False))
            else:
                pm0 = b0.prod.mark(1)
                p.emit("t", lambda e, mc=mc, tw=tw, t0=t0, sem=pm0[0]:
                       e.matmul(b0.t[0:1, 0:tw], onescol[:], xr(mc, t0, tw),
                                start=False, stop=True).then_inc(sem, 1))
                pm1 = b1.prod.mark(1)
                p.emit("t", lambda e, mc=mc, tw=tw, sem=pm1[0]: e.matmul(
                    b1.t[0:1, 0:tw], onescol[:],
                    r_ln[:, mc * tw:(mc + 1) * tw],
                    start=False, stop=True).then_inc(sem, 1))
        p.wait("v", b0.prod)
        p.wait("v", b1.prod)
        s1, s2 = b0.t[0:1, 0:tw], b1.t[0:1, 0:tw]
        vself(lambda e: e.tensor_scalar(mu, s1, 1.0 / D, None,
                                        op0=ALU.mult))
        vm0 = vself(lambda e: e.tensor_scalar(
            var, s2, 1.0 / D, None, op0=ALU.mult))
        b0.freed_by(vm0)
        b1.freed_by(vm0)
        vself(lambda e: e.tensor_tensor(nmr, mu, mu, op=ALU.mult))
        vself(lambda e: e.tensor_tensor(var, var, nmr, op=ALU.subtract))
        vself(lambda e: e.tensor_scalar(var, var, EPS_LN, None, op0=ALU.add))
        p.wait("a", c_vg)
        aq = c_ag.mark(1)
        p.emit("a", lambda e, sem=aq[0]: e.activation(
            nmr, var, AF.Sqrt).then_inc(sem, 1))
        p.wait("v", c_ag)
        vr = c_vg.mark(1)
        p.emit("v", lambda e, sem=vr[0]: e.reciprocal(
            rst, nmr).then_inc(sem, 1))
        p.wait("v", c_vg)
        vself(lambda e: e.tensor_tensor(nmr, mu, rst, op=ALU.mult))
        rm = c_row.mark(1)
        p.emit("v", lambda e, sem=rm[0]: e.tensor_scalar(
            nmr, nmr, -1.0, None, op0=ALU.mult).then_inc(sem, 1))
        p.wait("v", c_row)
        ln_apply_full(t0, tw, rst, nmr)
        return rst, nmr

    def ln_apply_full(t0, tw, rst_row, nmr_row):
        b2, b3 = B[2], B[3]
        b2.begin()
        b3.begin()
        p.wait("t", c_row)
        pm2 = b2.prod.mark(1)
        p.emit("t", lambda e, tw=tw, rr=rst_row, sem=pm2[0]: e.matmul(
            b2.t[:, 0:tw], ones1[:], rr, start=True,
            stop=True).then_inc(sem, 1))
        pm3 = b3.prod.mark(1)
        p.emit("t", lambda e, tw=tw, nr=nmr_row, sem=pm3[0]: e.matmul(
            b3.t[:, 0:tw], ones1[:], nr, start=True,
            stop=True).then_inc(sem, 1))
        p.wait("v", b2.prod)
        p.wait("v", b3.prod)
        for mc in range(NCH):
            p.emit("v", lambda e, mc=mc, tw=tw, t0=t0: e.tensor_tensor(
                r_ln[:, mc * tw:(mc + 1) * tw], xr(mc, t0, tw),
                b2.t[:, 0:tw], op=ALU.mult))
            if mc < NCH - 1:
                p.emit("v", lambda e, mc=mc, tw=tw: e.tensor_tensor(
                    r_ln[:, mc * tw:(mc + 1) * tw],
                    r_ln[:, mc * tw:(mc + 1) * tw], b3.t[:, 0:tw],
                    op=ALU.add))
            else:
                lm = c_ln.mark(1)
                p.emit("v", lambda e, mc=mc, tw=tw, sem=lm[0]:
                       e.tensor_tensor(
                           r_ln[:, mc * tw:(mc + 1) * tw],
                           r_ln[:, mc * tw:(mc + 1) * tw], b3.t[:, 0:tw],
                           op=ALU.add).then_inc(sem, 1))
                b2.freed_by(lm)
                b3.freed_by(lm)

    def ln_apply_cols(t0):
        for g in range(2):
            gb = (t0 // 128) + g
            b2, b3 = B[2], B[3]
            b2.begin()
            b3.begin()
            pm2 = b2.prod.mark(1)
            p.emit("t", lambda e, gb=gb, sem=pm2[0]: e.transpose(
                b2.t[0:1, 0:128], rst_col[:, gb:gb + 1],
                ident[:]).then_inc(sem, 1))
            pm3 = b3.prod.mark(1)
            p.emit("t", lambda e, gb=gb, sem=pm3[0]: e.transpose(
                b3.t[0:1, 0:128], nmr_col[:, gb:gb + 1],
                ident[:]).then_inc(sem, 1))
            p.wait("v", b2.prod)
            p.wait("v", b3.prod)
            p.wait("v", B[7].prod)
            p.emit("v", lambda e, b2=b2: e.tensor_copy(
                row2[0:1, 0:128], b2.t[0:1, 0:128]))
            cpm = c_vg.mark(1)
            p.emit("v", lambda e, b3=b3, sem=cpm[0]: e.tensor_copy(
                row2[0:1, 128:256], b3.t[0:1, 0:128]).then_inc(sem, 1))
            p.wait("t", c_vg)
            pm2b = b2.prod.mark(1)
            p.emit("t", lambda e, sem=pm2b[0]: e.matmul(
                b2.t[:, 0:128], ones1[:], row2[0:1, 0:128], start=True,
                stop=True).then_inc(sem, 1))
            pm3b = b3.prod.mark(1)
            p.emit("t", lambda e, sem=pm3b[0]: e.matmul(
                b3.t[:, 0:128], ones1[:], row2[0:1, 128:256], start=True,
                stop=True).then_inc(sem, 1))
            p.wait("v", b2.prod)
            p.wait("v", b3.prod)
            for mc in range(NCH):
                off = mc * TB + g * 128
                p.emit("v", lambda e, mc=mc, off=off, t0=t0, g=g:
                       e.tensor_tensor(
                           r_ln[:, off:off + 128],
                           xr(mc, t0 + g * 128, 128),
                           b2.t[:, 0:128], op=ALU.mult))
                if mc < NCH - 1:
                    p.emit("v", lambda e, mc=mc, off=off: e.tensor_tensor(
                        r_ln[:, off:off + 128], r_ln[:, off:off + 128],
                        b3.t[:, 0:128], op=ALU.add))
                else:
                    lm = c_ln.mark(1)
                    p.emit("v", lambda e, mc=mc, off=off, sem=lm[0]:
                           e.tensor_tensor(
                               r_ln[:, off:off + 128],
                               r_ln[:, off:off + 128], b3.t[:, 0:128],
                               op=ALU.add).then_inc(sem, 1))
                    b2.freed_by(lm)
                    b3.freed_by(lm)

    def store_stat_cols(ti, rst_row, nmr_row):
        b2, b3 = B[2], B[3]
        b2.begin()
        b3.begin()
        p.wait("t", c_row)
        pm2 = b2.prod.mark(1)
        p.emit("t", lambda e, rr=rst_row, sem=pm2[0]: e.transpose(
            b2.t[:, 0:1], rr, ident[0:1, 0:1]).then_inc(sem, 1))
        pm3 = b3.prod.mark(1)
        p.emit("t", lambda e, nr=nmr_row, sem=pm3[0]: e.transpose(
            b3.t[:, 0:1], nr, ident[0:1, 0:1]).then_inc(sem, 1))
        p.wait("v", b2.prod)
        p.wait("v", b3.prod)
        p.emit("v", lambda e, ti=ti, b2=b2: e.tensor_copy(
            rst_col[:, ti:ti + 1], b2.t[:, 0:1]))
        cm = c_vg.mark(1)
        p.emit("v", lambda e, ti=ti, b3=b3, sem=cm[0]: e.tensor_copy(
            nmr_col[:, ti:ti + 1], b3.t[:, 0:1]).then_inc(sem, 1))
        b2.freed_by(cm)
        b3.freed_by(cm)

    def router(t0):
        for blk in range(TT // 128):
            b4, b6 = B[4], B[6]
            b4.begin()
            p.wait("t", c_ln)
            for mc in range(NCH):
                last = mc == NCH - 1
                if not last:
                    p.emit("t", lambda e, mc=mc, blk=blk: e.matmul(
                        b4.t[:, 0:8],
                        r_ln[:, mc * TT + blk * 128: mc * TT + blk * 128
                             + 128],
                        gwt[:, mc * 8:(mc + 1) * 8],
                        start=(mc == 0), stop=False))
                else:
                    pm = b4.prod.mark(1)
                    p.emit("t", lambda e, mc=mc, blk=blk, sem=pm[0]:
                           e.matmul(
                               b4.t[:, 0:8],
                               r_ln[:, mc * TT + blk * 128: mc * TT +
                                    blk * 128 + 128],
                               gwt[:, mc * 8:(mc + 1) * 8],
                               start=False, stop=True).then_inc(sem, 1))
            p.wait("v", b4.prod)
            vm = vself(lambda e, b4=b4: e.tensor_copy(
                sc_tok[:, :], b4.t[:, 0:8]))
            b4.freed_by(vm)
            vself(lambda e: e.max(out=m8[:], in_=sc_tok[:]))
            vself(lambda e: e.tensor_tensor(
                d12[:], m8[:, 0:1], m8[:, 1:2], op=ALU.subtract))
            p.wait("a", c_vg)
            p.emit("a", lambda e: e.activation(w1c[:], d12[:], AF.Sigmoid))
            am = c_ag.mark(1)
            p.emit("a", lambda e, sem=am[0]: e.activation(
                w2c[:], d12[:], AF.Sigmoid, scale=-1.0).then_inc(sem, 1))
            p.wait("v", c_ag)
            vself(lambda e: e.tensor_scalar(
                indb[:], sc_tok[:], m8[:, 0:1], None, op0=ALU.is_equal))
            vself(lambda e: e.tensor_scalar(
                cmbt[:], indb[:], w1c[:], None, op0=ALU.mult))
            vself(lambda e: e.tensor_scalar(
                indb[:], sc_tok[:], m8[:, 1:2], None, op0=ALU.is_equal))
            vself(lambda e: e.tensor_scalar(
                indb[:], indb[:], w2c[:], None, op0=ALU.mult))
            cm = vself(lambda e: e.tensor_add(cmbt[:], cmbt[:], indb[:]))
            b6.begin()
            p.wait("t", c_vg)
            pm6 = b6.prod.mark(1)
            p.emit("t", lambda e, sem=pm6[0]: e.transpose(
                b6.t[0:8, 0:128], cmbt[:], ident[:]).then_inc(sem, 1))
            p.wait("v", b6.prod)
            vm6 = c_vg.mark(1)
            p.emit("v", lambda e, blk=blk, sem=vm6[0]: e.tensor_copy(
                comb[:, blk * 128:(blk + 1) * 128],
                b6.t[0:8, 0:128]).then_inc(sem, 1))
            b6.freed_by(vm6)
        return (c_vg.h, c_vg.n)

    def dense_matmul_chain(bank, wslices_src, rhs_fn, nk, out_cols,
                           consumer_key):
        i, ld = wload(wslices_src, nk * 128)
        bank.begin()
        p.wait("t", ld)
        for k in range(nk):
            last = k == nk - 1
            if not last:
                p.emit("t", lambda e, k=k, i=i, bank=bank, oc=out_cols,
                       rhs=rhs_fn: e.matmul(
                           bank.t[:, 0:oc], wb[i][:, k * 128:(k + 1) * 128],
                           rhs(k), start=(k == 0), stop=False))
            else:
                pm = bank.prod.mark(1)
                p.emit("t", lambda e, k=k, i=i, bank=bank, oc=out_cols,
                       rhs=rhs_fn, sem=pm[0]: e.matmul(
                           bank.t[:, 0:oc], wb[i][:, k * 128:(k + 1) * 128],
                           rhs(k), start=(k == 0), stop=True)
                       .then_inc(sem, 1))
                wfree[i] = pm
        p.wait(consumer_key, bank.prod)

    def expert(eidx, t0, comb_evt):
        b0, b1, b2 = B[0], B[1], B[2]
        b5 = B[5]
        b5.begin()
        p.wait("t", comb_evt)
        pm5 = b5.prod.mark(1)
        p.emit("t", lambda e, eidx=eidx, sem=pm5[0]: e.matmul(
            b5.t[0:1, 0:TT], ident[0:8, eidx:eidx + 1], comb[0:8, :],
            start=True, stop=True).then_inc(sem, 1))
        p.wait("v", b5.prod)
        crm = c_vg.mark(1)
        p.emit("v", lambda e, b5=b5, sem=crm[0]: e.tensor_copy(
            row2[0:1, 0:TT], b5.t[0:1, 0:TT]).then_inc(sem, 1))
        b5.freed_by(crm)
        b0.begin()
        p.wait("t", c_vg)
        pm0 = b0.prod.mark(1)
        p.emit("t", lambda e, sem=pm0[0]: e.matmul(
            b0.t[:, 0:TT], ones1[:], row2[0:1, 0:TT],
            start=True, stop=True).then_inc(sem, 1))
        p.wait("v", b0.prod)
        for hc in range(HCH):
            dense_matmul_chain(
                b1, wg_d[eidx, hc],
                lambda k: r_ln[:, k * TT:(k + 1) * TT], NCH, TT, "a")
            hm = c_hs.mark(1)
            p.emit("a", lambda e, hc=hc, sem=hm[0]: e.activation(
                r_h[:, hc * TT:(hc + 1) * TT], b1.t[:, 0:TT],
                AF.Silu).then_inc(sem, 1))
            b1.freed_by(hm)
            dense_matmul_chain(
                b2, wu_d[eidx, hc],
                lambda k: r_ln[:, k * TT:(k + 1) * TT], NCH, TT, "v")
            p.wait("v", c_hs)
            um = c_vg.mark(1)
            p.emit("v", lambda e, hc=hc, sem=um[0]: e.tensor_tensor(
                r_h[:, hc * TT:(hc + 1) * TT], r_h[:, hc * TT:(hc + 1) * TT],
                b2.t[:, 0:TT], op=ALU.mult).then_inc(sem, 1))
            b2.freed_by(um)
            hp = c_vg.mark(1)
            p.emit("v", lambda e, hc=hc, sem=hp[0]: e.tensor_tensor(
                r_h[:, hc * TT:(hc + 1) * TT], r_h[:, hc * TT:(hc + 1) * TT],
                b0.t[:, 0:TT], op=ALU.mult).then_inc(sem, 1))
        b0.freed_by((c_vg.h, c_vg.n))
        hprime_evt = (c_vg.h, c_vg.n)
        for dc in range(NCH):
            bd = B[3 + (dc % 2)]
            p.wait("t", hprime_evt)
            dense_matmul_chain(
                bd, wd_d[eidx, dc],
                lambda k: r_h[:, k * TT:(k + 1) * TT], HCH, TT, "v")
            em = c_vg.mark(1)
            p.emit("v", lambda e, bd=bd, dc=dc, t0=t0, sem=em[0]:
                   e.scalar_tensor_tensor(
                       xr(dc, t0, TT), bd.t[:, 0:TT], 0.5, xr(dc, t0, TT),
                       op0=ALU.mult, op1=ALU.add).then_inc(sem, 1))
            bd.freed_by(em)

    def shared_expert(t0, sub):
        tw = 256
        tof = t0 + sub * tw
        loff = sub * tw
        b1, b2 = B[1], B[2]
        for hsc in range(NCH):
            dense_matmul_chain(
                b1, sg_d[hsc],
                lambda k: r_ln[:, k * TT + loff: k * TT + loff + tw],
                NCH, tw, "a")
            hm = c_hs.mark(1)
            p.emit("a", lambda e, hsc=hsc, sem=hm[0]: e.activation(
                r_h[:, hsc * tw:(hsc + 1) * tw], b1.t[:, 0:tw],
                AF.Silu).then_inc(sem, 1))
            b1.freed_by(hm)
            dense_matmul_chain(
                b2, su_d[hsc],
                lambda k: r_ln[:, k * TT + loff: k * TT + loff + tw],
                NCH, tw, "v")
            p.wait("v", c_hs)
            um = c_vg.mark(1)
            p.emit("v", lambda e, hsc=hsc, sem=um[0]: e.tensor_tensor(
                r_h[:, hsc * tw:(hsc + 1) * tw],
                r_h[:, hsc * tw:(hsc + 1) * tw], b2.t[:, 0:tw],
                op=ALU.mult).then_inc(sem, 1))
            b2.freed_by(um)
        hprime_evt = (c_vg.h, c_vg.n)
        b5, b0 = B[5], B[0]
        b5.begin()
        p.wait("t", c_ln)
        for mc in range(NCH):
            if mc < NCH - 1:
                p.emit("t", lambda e, mc=mc: e.matmul(
                    b5.t[0:1, 0:tw], set_[:, mc:mc + 1],
                    r_ln[:, mc * TT + loff: mc * TT + loff + tw],
                    start=(mc == 0), stop=False))
            else:
                pm = b5.prod.mark(1)
                p.emit("t", lambda e, mc=mc, sem=pm[0]: e.matmul(
                    b5.t[0:1, 0:tw], set_[:, mc:mc + 1],
                    r_ln[:, mc * TT + loff: mc * TT + loff + tw],
                    start=False, stop=True).then_inc(sem, 1))
        p.wait("a", b5.prod)
        sm = c_ag.mark(1)
        p.emit("a", lambda e, sem=sm[0]: e.activation(
            row2[:, 0:tw], b5.t[0:1, 0:tw], AF.Sigmoid).then_inc(sem, 1))
        b5.freed_by(sm)
        b0.begin()
        p.wait("t", c_ag)
        pm0 = b0.prod.mark(1)
        p.emit("t", lambda e, sem=pm0[0]: e.matmul(
            b0.t[:, 0:tw], ones1[:], row2[:, 0:tw], start=True,
            stop=True).then_inc(sem, 1))
        p.wait("v", b0.prod)
        sgm = c_vg.mark(1)
        p.emit("v", lambda e, sem=sgm[0]: e.tensor_copy(
            sgrep[:, 0:tw], b0.t[:, 0:tw]).then_inc(sem, 1))
        b0.freed_by(sgm)
        for dc in range(NCH):
            bd = B[3 + (dc % 2)]
            p.wait("t", hprime_evt)
            dense_matmul_chain(
                bd, sd_d[dc],
                lambda k: r_h[:, k * tw:(k + 1) * tw], NCH, tw, "v")
            p.emit("v", lambda e, bd=bd: e.tensor_tensor(
                tsc[:, 0:tw], bd.t[:, 0:tw], sgrep[:, 0:tw], op=ALU.mult))
            em = c_vg.mark(1)
            p.emit("v", lambda e, dc=dc, tof=tof, sem=em[0]:
                   e.scalar_tensor_tensor(
                       xr(dc, tof, tw), tsc[:, 0:tw], 0.5, xr(dc, tof, tw),
                       op0=ALU.mult, op1=ALU.add).then_inc(sem, 1))
            bd.freed_by(em)

    def ffn_call():
        for ti in range(NT):
            t0 = ti * TT
            ln_stats_and_ln(t0, TT, (kv, 0),
                            ereaders=(B[5].prod.h, B[5].prod.n))
            comb_evt = router(t0)
            for eidx in range(NE):
                expert(eidx, t0, comb_evt)
            for sub in range(2):
                shared_expert(t0, sub)
        last_xw[0] = (c_vg.h, c_vg.n)

    def attn():
        km = c_vg.mark(1)
        p.emit("v", lambda e, sem=km[0]: e.memset(kv[:, :], 0.0)
               .then_inc(sem, 1))
        om = c_vg.mark(1)
        p.emit("v", lambda e, sem=om[0]: e.memset(
            r_h.ap()[:, 2048:2048 + 2064].rearrange(
                "p (h q) -> p h q", h=NCH)[:, :, 128:129],
            1.0).then_inc(sem, 1))
        for ti in range(NCH):
            t0 = ti * 128
            tw = 128
            rr, nr = ln_stats_and_ln(t0, tw, (r_ln, 4096),
                                     ereaders=(B[5].prod.h, B[5].prod.n))
            store_stat_cols(ti, rr, nr)
            for proj in range(2):
                wsl = wk_d if proj == 0 else wv_d
                for ds in range(4):
                    bp = B[4 + proj]
                    bp.begin()
                    p.wait("t", c_ln)
                    for mc in range(NCH):
                        i, ld = wload(wsl[mc, ds], 512)
                        p.wait("t", ld)
                        um = c_mmu.mark(1)
                        p.emit("t", lambda e, mc=mc, i=i, bp=bp, sem=um[0]:
                               e.matmul(
                                   bp.t[:, 0:512],
                                   r_ln[:, mc * tw: mc * tw + 128],
                                   wb[i][:, 0:512], start=(mc == 0),
                                   stop=False).then_inc(sem, 1))
                        wfree[i] = um
                    i, ld = wload(wsl[NCH, ds], 512)
                    p.wait("t", ld)
                    pm = bp.prod.mark(1)
                    p.emit("t", lambda e, i=i, bp=bp, sem=pm[0]: e.matmul(
                        bp.t[:, 0:512], ones1[:], wb[i][0:1, 0:512],
                        start=False, stop=True).then_inc(sem, 1))
                    wfree[i] = pm
                    if proj == 0:
                        p.wait("a", bp.prod)
                        p.wait("a", decay_ready)
                        kmn = c_ag.mark(1)
                        p.emit("a", lambda e, ds=ds, ti=ti, bp=bp,
                               sem=kmn[0]: e.activation(
                                   r_h[:, ds * 512:(ds + 1) * 512],
                                   bp.t[:, 0:512], AF.Relu,
                                   scale=dec_col[:, ti:ti + 1])
                               .then_inc(sem, 1))
                        bp.freed_by(kmn)
                    else:
                        p.wait("v", bp.prod)
                        vmn = c_vg.mark(1)
                        p.emit("v", lambda e, ds=ds, bp=bp, sem=vmn[0]:
                               e.tensor_copy(
                                   r_h.ap()[:, 2048 + ds * 516:
                                            2048 + (ds + 1) * 516]
                                   .rearrange("p (h q) -> p h q", h=4)
                                   [:, :, 0:128],
                                   bp.t[:, 0:512].rearrange(
                                       "p (h q) -> p h q", h=4))
                               .then_inc(sem, 1))
                        bp.freed_by(vmn)
            p.wait("t", c_ag)
            p.wait("t", c_vg)
            if DBG and ti == 0:
                p.wait("g", c_ag)
                p.wait("g", c_vg)
                p.wait("g", c_ln)
                dm0 = c_st.mark(16)
                p.emit("g", lambda e, sem=dm0[0]: e.dma_start(
                    out=dbg_out[0, :, :], in_=r_h[:, 0:4096])
                    .then_inc(sem, 16))
                dm1 = c_st.mark(16)
                p.emit("g", lambda e, sem=dm1[0]: e.dma_start(
                    out=dbg_out[1, :, 0:2064], in_=r_h[:, 2048:4112])
                    .then_inc(sem, 16))
                p.wait("t", (c_st.h, c_st.n))
                p.wait("v", (c_st.h, c_st.n))
                p.wait("a", (c_st.h, c_st.n))
            for h in range(NCH):
                bkv = B[6 + (h % 2)]
                bkv.begin()
                pm = bkv.prod.mark(1)
                p.emit("t", lambda e, h=h, bkv=bkv, sem=pm[0]: e.matmul(
                    bkv.t[:, 0:129], r_h[:, h * 128:(h + 1) * 128],
                    r_h[:, 2048 + h * 129: 2048 + (h + 1) * 129],
                    start=True, stop=True).then_inc(sem, 1))
                p.wait("v", bkv.prod)
                if DBG and ti == 15 and h == 0:
                    dcp = c_vg.mark(1)
                    p.emit("v", lambda e, bkv=bkv, sem=dcp[0]: e.tensor_copy(
                        tsc[:, 0:129], bkv.t[:, 0:129]).then_inc(sem, 1))
                    p.wait("g", (c_vg.h, c_vg.n))
                    dps = c_st.mark(16)
                    p.emit("g", lambda e, sem=dps[0]: e.dma_start(
                        out=dbg_out[4, :, 0:129], in_=tsc[:, 0:129])
                        .then_inc(sem, 16))
                    p.wait("v", (c_st.h, c_st.n))
                am_ = c_vg.mark(1)
                p.emit("v", lambda e, h=h, bkv=bkv, sem=am_[0]:
                       e.tensor_tensor(
                           kv[:, h * 129:(h + 1) * 129],
                           kv[:, h * 129:(h + 1) * 129],
                           bkv.t[:, 0:129], op=ALU.add).then_inc(sem, 1))
                bkv.freed_by(am_)
            if DBG and ti == 0:
                p.wait("g", (c_vg.h, c_vg.n))
                dmK = c_st.mark(16)
                p.emit("g", lambda e, sem=dmK[0]: e.dma_start(
                    out=dbg_out[2, :, 0:2064], in_=kv[:, :]).then_inc(sem, 16))
                p.wait("t", (c_st.h, c_st.n))
                p.wait("v", (c_st.h, c_st.n))
                p.wait("a", (c_st.h, c_st.n))
        p.wait("g", (c_vg.h, c_vg.n))
        g1 = c_cd.mark(16)
        p.emit("g", lambda e, sem=g1[0]: e.dma_start(
            out=cc_in[:, :], in_=kv[:, :]).then_inc(sem, 16))
        p.wait("g", c_cd)
        g2 = c_cc.mark(1)
        p.emit("g", lambda e, sem=g2[0]: e.collective_compute(
            "AllReduce", ALU.add,
            replica_groups=[[0, 1, 2, 3], [4, 5, 6, 7]],
            ins=[cc_in[:, :]], outs=[cc_out[:, :]]).then_inc(sem, 1))
        p.wait("g", c_cc)
        g3 = c_cd.mark(16)
        p.emit("g", lambda e, sem=g3[0]: e.dma_start(
            out=kv[:, :], in_=cc_out[:, :]).then_inc(sem, 16))
        kv_ready = (c_cd.h, c_cd.n)
        if DBG:
            p.wait("g", kv_ready)
            dm3 = c_st.mark(16)
            p.emit("g", lambda e, sem=dm3[0]: e.dma_start(
                out=dbg_out[3, :, 0:2064], in_=kv[:, :]).then_inc(sem, 16))
        loopa_done = (c_vg.h, c_vg.n)
        for ti in range(NTB):
            t0 = ti * TB
            p.wait("t", loopa_done)
            ln_apply_cols(t0)
            p.wait("g", B[2].prod)
            p.wait("g", B[3].prod)
            gdm = c_gg.mark(1)
            p.emit("g", lambda e, t0=t0, sem=gdm[0]: e.iota(
                row2[0:1, 0:TB], pattern=[[1, TB]], base=t0,
                channel_multiplier=0,
                allow_small_or_imprecise_dtypes=True).then_inc(sem, 1))
            p.wait("a", c_gg)
            adm = c_ag.mark(1)
            p.emit("a", lambda e, sem=adm[0]: e.activation(
                row2[0:1, 0:TB], row2[0:1, 0:TB], AF.Exp, bias=pbias[:],
                scale=-1.0 / NSEQ).then_inc(sem, 1))
            b7 = B[7]
            b7.begin()
            p.wait("t", c_ag)
            pm7 = b7.prod.mark(1)
            p.emit("t", lambda e, sem=pm7[0]: e.matmul(
                b7.t[:, 0:TB], ones1[:], row2[0:1, 0:TB],
                start=True, stop=True).then_inc(sem, 1))
            p.wait("v", b7.prod)
            for dc in range(NCH):
                bq_ = B[dc % 2]
                p.wait("t", c_ln)
                dense_matmul_chain(
                    bq_, wq_d[dc],
                    lambda k: r_ln[:, k * TB:(k + 1) * TB], NCH, TB, "a")
                qm = c_ag.mark(1)
                p.emit("a", lambda e, dc=dc, bq_=bq_, sem=qm[0]:
                       e.activation(
                           r_h[:, dc * TB:(dc + 1) * TB], bq_.t[:, 0:TB],
                           AF.Relu, bias=bqct[:, dc:dc + 1])
                       .then_inc(sem, 1))
                bq_.freed_by(qm)
                p.wait("v", c_ag)
                vq = c_vg.mark(1)
                p.emit("v", lambda e, dc=dc, b7=b7, sem=vq[0]:
                       e.tensor_tensor(
                           r_h[:, dc * TB:(dc + 1) * TB],
                           r_h[:, dc * TB:(dc + 1) * TB],
                           b7.t[:, 0:TB], op=ALU.mult).then_inc(sem, 1))
            b7.freed_by((c_vg.h, c_vg.n))
            q_evt = (c_vg.h, c_vg.n)
            p.wait("t", q_evt)
            p.wait("t", kv_ready)
            for h in range(NCH):
                for nb in range(2):
                    ba = B[2 + ((h * 2 + nb) % 2)]
                    bt_ = B[4 + ((h * 2 + nb) % 2)]
                    ba.begin()
                    pma = ba.prod.mark(1)
                    p.emit("t", lambda e, h=h, nb=nb, ba=ba, sem=pma[0]:
                           e.matmul(
                               ba.t[:, 0:129],
                               r_h[:, h * TB + nb * 128:
                                   h * TB + nb * 128 + 128],
                               kv[:, h * 129:(h + 1) * 129],
                               start=True, stop=True).then_inc(sem, 1))
                    p.wait("v", ba.prod)
                    vself(lambda e, ba=ba: e.tensor_scalar(
                        d12[:], ba.t[:, 128:129], EPS_ATT, None,
                        op0=ALU.add))
                    rm_ = c_vg.mark(1)
                    p.emit("v", lambda e, sem=rm_[0]: e.reciprocal(
                        w1c[:], d12[:]).then_inc(sem, 1))
                    p.wait("v", c_vg)
                    zm = c_vg.mark(1)
                    p.emit("v", lambda e, ba=ba, sem=zm[0]: e.tensor_scalar(
                        tsc[:, 0:128], ba.t[:, 0:128], w1c[:], None,
                        op0=ALU.mult).then_inc(sem, 1))
                    ba.freed_by(zm)
                    bt_.begin()
                    p.wait("t", c_vg)
                    pmt = bt_.prod.mark(1)
                    p.emit("t", lambda e, bt_=bt_, sem=pmt[0]: e.transpose(
                        bt_.t[:, 0:128], tsc[:, 0:128],
                        ident[:]).then_inc(sem, 1))
                    p.wait("v", bt_.prod)
                    vm_ = c_vg.mark(1)
                    p.emit("v", lambda e, h=h, nb=nb, bt_=bt_, sem=vm_[0]:
                           e.tensor_copy(
                               r_ln[:, NCH * TB + h * TB + nb * 128:
                                    NCH * TB + h * TB + nb * 128 + 128],
                               bt_.t[:, 0:128]).then_inc(sem, 1))
                    bt_.freed_by(vm_)
            az_evt = (c_vg.h, c_vg.n)
            if DBG and ti == 0:
                p.wait("g", q_evt)
                dm4 = c_st.mark(16)
                p.emit("g", lambda e, sem=dm4[0]: e.dma_start(
                    out=dbg_out[4, :, :], in_=r_h[:, 0:4096])
                    .then_inc(sem, 16))
                p.wait("g", az_evt)
                dm5 = c_st.mark(16)
                p.emit("g", lambda e, sem=dm5[0]: e.dma_start(
                    out=dbg_out[5, :, :], in_=r_ln[:, 4096:8192])
                    .then_inc(sem, 16))
                p.wait("t", (c_st.h, c_st.n))
                p.wait("v", (c_st.h, c_st.n))
                p.wait("a", (c_st.h, c_st.n))
            p.wait("t", az_evt)
            for dc in range(NCH):
                by = B[6 + (dc % 2)]
                dense_matmul_chain(
                    by, wo_d[dc],
                    lambda k: r_ln[:, NCH * TB + k * TB:
                                   NCH * TB + (k + 1) * TB], NCH, TB, "v")
                ym = c_vg.mark(1)
                p.emit("v", lambda e, by=by, dc=dc, t0=t0, sem=ym[0]:
                       e.scalar_tensor_tensor(
                           xr(dc, t0, TB), by.t[:, 0:TB],
                           boct[:, dc:dc + 1], xr(dc, t0, TB),
                           op0=ALU.add, op1=ALU.add).then_inc(sem, 1))
                by.freed_by(ym)
        last_xw[0] = (c_vg.h, c_vg.n)


    def out_phase():
        p.wait("t", last_xw[0])
        for tb in range(NCH):
            half = tb % 2
            if tb >= 2:
                p.wait("v", (c_st.h, (tb - 1) * 16))
            for mc in range(NCH):
                bk = B[mc % 2]
                bk.begin()
                pm = bk.prod.mark(1)
                p.emit("t", lambda e, bk=bk, mc=mc, tb=tb, sem=pm[0]:
                       e.transpose(
                           bk.t[:, 0:128],
                           x_res[:, mc * T + tb * 128: mc * T +
                                 (tb + 1) * 128],
                           ident[:]).then_inc(sem, 1))
                p.wait("v", bk.prod)
                vm = c_vg.mark(1)
                p.emit("v", lambda e, bk=bk, mc=mc, half=half, sem=vm[0]:
                       e.tensor_copy(
                           r_h[:, half * D + mc * 128:
                               half * D + (mc + 1) * 128],
                           bk.t[:, 0:128]).then_inc(sem, 1))
                bk.freed_by(vm)
            p.wait("g", (c_vg.h, c_vg.n))
            sm_ = c_st.mark(16)
            p.emit("g", lambda e, tb=tb, half=half, sem=sm_[0]: e.dma_start(
                out=y_out[tb * 128:(tb + 1) * 128, :],
                in_=r_h[:, half * D: half * D + D]).then_inc(sem, 16))
        p.wait("g", c_st)

    if PH >= 1:
        ffn_call()
    if PH >= 2:
        attn()
    if PH >= 3:
        ffn_call()
    out_phase()

    with nc.Block() as block:
        @block.sync
        def _(eng):
            for fn in p.q["y"]:
                fn(eng)

        @block.tensor
        def _(eng):
            for fn in p.q["t"]:
                fn(eng)

        @block.vector
        def _(eng):
            for fn in p.q["v"]:
                fn(eng)

        @block.scalar
        def _(eng):
            for fn in p.q["a"]:
                fn(eng)

        @block.gpsimd
        def _(eng):
            for fn in p.q["g"]:
                fn(eng)

    return nc


def _pack_std(wT):
    dn, do = wT.shape
    a = wT.reshape(dn // 128, 128, do // 128, 128)
    a = a.transpose(2, 1, 0, 3)
    return np.ascontiguousarray(a.reshape(do // 128, 128, (dn // 128) * 128))


def _pack_swap(wT, bias):
    a = wT.reshape(NCH, 128, 4, 512).transpose(0, 2, 1, 3)
    out = np.zeros((NCH + 1, 4, 128, 512), dtype=wT.dtype)
    out[:NCH] = a
    out[NCH, :, 0, :] = bias.reshape(4, 512)
    return np.ascontiguousarray(out)


def kernel(L0_attn_x, L0_attn_Wq, L0_attn_bq, L0_attn_Wk, L0_attn_bk,
           L0_attn_Wv, L0_attn_bv, L0_attn_Wo, L0_attn_bo, L0_ffn_gate_w,
           L0_ffn_w_gate, L0_ffn_w_up, L0_ffn_w_down, L0_ffn_sh_gate,
           L0_ffn_sh_up, L0_ffn_sh_down, L0_ffn_se_gate_w):
    f = np.float32
    x = np.ascontiguousarray(np.asarray(L0_attn_x, dtype=f))
    wq = _pack_std(np.ascontiguousarray(np.asarray(L0_attn_Wq, dtype=f).T))
    wo = _pack_std(np.ascontiguousarray(np.asarray(L0_attn_Wo, dtype=f).T))
    wk = _pack_swap(np.ascontiguousarray(np.asarray(L0_attn_Wk, dtype=f).T),
                    np.asarray(L0_attn_bk, dtype=f))
    wv = _pack_swap(np.ascontiguousarray(np.asarray(L0_attn_Wv, dtype=f).T),
                    np.asarray(L0_attn_bv, dtype=f))
    wg = np.stack([_pack_std(np.ascontiguousarray(
        np.asarray(L0_ffn_w_gate[e], dtype=f).T)) for e in range(NE)])
    wu = np.stack([_pack_std(np.ascontiguousarray(
        np.asarray(L0_ffn_w_up[e], dtype=f).T)) for e in range(NE)])
    wd = np.stack([_pack_std(np.ascontiguousarray(
        np.asarray(L0_ffn_w_down[e], dtype=f).T)) for e in range(NE)])
    sg2 = _pack_std(np.ascontiguousarray(np.asarray(L0_ffn_sh_gate,
                                                    dtype=f).T))
    su2 = _pack_std(np.ascontiguousarray(np.asarray(L0_ffn_sh_up, dtype=f).T))
    sd2 = _pack_std(np.ascontiguousarray(np.asarray(L0_ffn_sh_down,
                                                    dtype=f).T))
    gwT = np.ascontiguousarray(np.asarray(L0_ffn_gate_w, dtype=f).T)
    gw = np.ascontiguousarray(gwT.reshape(NCH, 128, 8).transpose(1, 0, 2))
    seT = np.asarray(L0_ffn_se_gate_w, dtype=f).reshape(D)
    se = np.ascontiguousarray(seT.reshape(NCH, 128).T)
    bqc = np.ascontiguousarray(np.asarray(L0_attn_bq, dtype=f)
                               .reshape(NCH, 128).T)
    boc = np.ascontiguousarray(np.asarray(L0_attn_bo, dtype=f)
                               .reshape(NCH, 128).T)

    nc = build_program()
    in_maps = []
    for c in range(8):
        b, q = c // 4, c % 4
        xs = np.ascontiguousarray(x[b, q * T:(q + 1) * T, :])
        in_maps.append(dict(
            x=xs, wq=wq, wo=wo, wk=wk, wv=wv, wg=wg, wu=wu, wd=wd,
            sg=sg2, su=su2, sd=sd2, gw=gw, se=se, bqc=bqc, boc=boc,
            poff=np.array([[q * T]], dtype=f)))
    res = run_bass_kernel_spmd(nc, in_maps, core_ids=list(range(8)))
    out = np.empty((2, 8192, D), dtype=f)
    for c in range(8):
        b, q = c // 4, c % 4
        out[b, q * T:(q + 1) * T, :] = res.results[c]["y"]
    return out
